# version 05 (final)
# baseline (speedup 1.0000x reference)
import numpy as np
import concourse.bass as bass
import concourse.mybir as mybir
from concourse.ap import AP
from concourse.bass_utils import run_bass_kernel_spmd
from contextlib import ExitStack

F32 = mybir.dt.float32
BF16 = mybir.dt.bfloat16
I32 = mybir.dt.int32
AF = mybir.ActivationFunctionType
ALU = mybir.AluOpType
AX = mybir.AxisListType

S, D, DC, NW, WN, NT = 2048, 1024, 8, 4, 512, 16
H, HD, NP, LAT, LC = 16, 64, 8, 256, 2
HID, HC0, EHID, SHID, E = 3072, 24, 512, 1024, 8
QS, DEQ, EPS = 0.125, 1.0 / 127.0, 1e-5
NSLOT = 8


class Prog:
    PARANOID = 2

    def __init__(self, sems, active, eng):
        self.sems = sems
        self.active = active
        self.eng = eng
        self.cnt = {k: 0 for k in sems}
        self.lastw = {}

    def wait(self, engine, toks):
        for t in toks:
            if t is None:
                continue
            s, v = t
            if v <= 0:
                continue
            key = (engine, s)
            if self.lastw.get(key, 0) >= v:
                continue
            self.lastw[key] = v
            if engine == self.active:
                self.eng.wait_ge(self.sems[s], v)

    def do(self, engine, fn, wait=(), inc=None):
        if self.PARANOID == 1 or (self.PARANOID == 2 and
                                  engine in ("sync", "gpsimd")) or \
                (self.PARANOID == 3 and engine == "gpsimd") or \
                (self.PARANOID == 4 and engine == "sync"):
            wait = tuple(wait) + tuple((k, v) for k, v in self.cnt.items())
        self.wait(engine, wait)
        if engine == self.active:
            ins = fn(self.eng)
            if inc is not None:
                ins.then_inc(self.sems[inc[0]], inc[1])
        if inc is not None:
            self.cnt[inc[0]] += inc[1]
            return (inc[0], self.cnt[inc[0]])
        return None

    def tok(self, sem):
        return (sem, self.cnt[sem])


def build_nc(debug=0, phases=99):
    nc = bass.Bass("TRN2", target_bir_lowering=False)

    def dt_in(name, shape, dt=F32):
        return nc.dram_tensor(name, shape, dt, kind="ExternalInput").ap()

    x_in = dt_in("x", [S, D])
    Wq = dt_in("Wq", [D, D]); bq_i = dt_in("bq", [D])
    Wl = dt_in("Wl", [D, LAT]); bl_i = dt_in("bl", [LAT])
    Wk = dt_in("Wk", [LAT, D])
    Wv = dt_in("Wv", [LAT, D]); bv_i = dt_in("bv", [D])
    Wo = dt_in("Wo", [D, D]); bo_i = dt_in("bo", [D])
    wg0 = dt_in("wg0", [HID, D], I32)
    wu0 = dt_in("wu0", [HID, D], I32)
    wd0 = dt_in("wd0", [D, HID], I32)
    Wqkv = dt_in("Wqkv", [D, 3 * D]); bqkv = dt_in("bqkv", [3 * D])
    Wout = dt_in("Wout", [D, D]); bout_i = dt_in("bout", [D])
    relk = dt_in("relk", [129, HD]); relv = dt_in("relv", [129, HD])
    gw = dt_in("gw", [D, E]); gb_i = dt_in("gb", [E])
    wg1 = dt_in("wg1", [E, D, EHID]); wu1 = dt_in("wu1", [E, D, EHID])
    wd1 = dt_in("wd1", [E, EHID, D])
    swg = dt_in("swg", [D, SHID]); swu = dt_in("swu", [D, SHID])
    swd = dt_in("swd", [SHID, D])
    idmi = dt_in("idm", [128, 128], F32)
    sh64i = dt_in("sh64", [64, 128], F32)
    seli = dt_in("selw", [129, 384], F32)
    y_out = nc.dram_tensor("y", [S, D], F32, kind="ExternalOutput").ap()

    xbf = nc.dram_tensor("xbf", [S, D], BF16, kind="Internal").ap()
    wg0bf = nc.dram_tensor("wg0bf", [HID, D], BF16, kind="Internal").ap()
    wu0bf = nc.dram_tensor("wu0bf", [HID, D], BF16, kind="Internal").ap()
    wd0bf = nc.dram_tensor("wd0bf", [D, HID], BF16, kind="Internal").ap()
    wg0T = nc.dram_tensor("wg0T", [D, HID], BF16, kind="Internal").ap()
    wu0T = nc.dram_tensor("wu0T", [D, HID], BF16, kind="Internal").ap()
    wd0T = nc.dram_tensor("wd0T", [HID, D], BF16, kind="Internal").ap()
    skew = nc.dram_tensor("skew", [NSLOT * 128 * 512], BF16, kind="Internal").ap()
    skew2 = nc.dram_tensor("skew2", [NSLOT * 128 * 640], BF16, kind="Internal").ap()

    dbg = {}
    if debug:
        dbg["dxA"] = nc.dram_tensor("dxA", [128, DC * S], BF16, kind="ExternalOutput").ap()
        dbg["dxB"] = nc.dram_tensor("dxB", [128, DC * S], BF16, kind="ExternalOutput").ap()
        dbg["dq"] = nc.dram_tensor("dq", [128, S], BF16, kind="ExternalOutput").ap()
        dbg["dk"] = nc.dram_tensor("dk", [128, S], BF16, kind="ExternalOutput").ap()
        dbg["dv"] = nc.dram_tensor("dv", [128, NT * 1040], BF16, kind="ExternalOutput").ap()
        dbg["dctx"] = nc.dram_tensor("dctx", [128, DC * S], BF16, kind="ExternalOutput").ap()
        dbg["dhb"] = nc.dram_tensor("dhb", [128, HC0 * WN], BF16, kind="ExternalOutput").ap()
        dbg["drw"] = nc.dram_tensor("drw", [8, S], F32, kind="ExternalOutput").ap()
        dbg["dwt"] = nc.dram_tensor("dwt", [256, HID], BF16, kind="ExternalOutput").ap()
        dbg["dct"] = nc.dram_tensor("dct", [128, 128], BF16, kind="ExternalOutput").ap()
        dbg["dst2"] = nc.dram_tensor("dst2", [128, 128], F32, kind="ExternalOutput").ap()
        for nm in ("dx1", "dx2", "dx3"):
            dbg[nm] = nc.dram_tensor(nm, [128, DC * S], BF16, kind="ExternalOutput").ap()

    with ExitStack() as ctx:
        ctx.enter_context(nc.allow_low_precision(
            "bf16 matmul operands + bf16 row intermediates; fp32 accumulation "
            "everywhere; within this problem's stated tolerance"))
        def sbt(name, shape, dt=F32):
            return ctx.enter_context(nc.sbuf_tensor(name, shape, dt))

        def psb(name, shape, dt=F32):
            return ctx.enter_context(nc.psum_tensor(name, shape, dt))

        xA = sbt("xA", [128, DC * S], BF16)
        xB = sbt("xB", [128, DC * S], BF16)
        v65 = sbt("v65", [128, NT * 1040], BF16)
        qp = sbt("qp", [128, S], BF16)
        kp = sbt("kp", [128, S], BF16)
        ctxall = sbt("ctxall", [128, DC * S], BF16)
        hbuf = sbt("hbuf", [128, HC0 * WN], BF16)
        wsA = sbt("wsA", [128, DC * 128], BF16)
        wsB = sbt("wsB", [128, DC * 128], BF16)
        wsC = sbt("wsC", [128, DC * 128], BF16)
        wsD = sbt("wsD", [128, DC * 128], BF16)
        wmA = hbuf[:, 0:DC * WN]
        wmB = hbuf[:, DC * WN:2 * DC * WN]
        latb = hbuf[:, 2 * DC * WN:2 * DC * WN + LC * S]
        qwin = hbuf[:, 0:384]
        atx = hbuf[:, 384:1920]
        selb = hbuf[:, 1920:2304]
        selr = hbuf[0:1, 2304:2688]
        attnT = sbt("attnT", [128, 2 * WN], BF16)
        pext = sbt("pext", [128, 512], BF16)
        rscrB = sbt("rscrB", [128, 26])
        rscr2B = sbt("rscr2B", [128, 16])
        bwin = sbt("bwin", [128, 4 * 384], BF16)
        eext = sbt("eext", [128, 512], BF16)
        e128rep = sbt("e128rep", [128, 128], BF16)
        evd = sbt("evd", [128, 3 * 64], BF16)
        ev128d = sbt("ev128d", [1, 64], BF16)

        tmpO = sbt("tmpO", [64, WN], BF16)

        sq = sbt("sq", [128, 2 * WN], BF16)
        reprow = sq
        outt = sbt("outt", [128, D])
        h32 = outt[:, 0:WN]
        gsb = outt[:, WN:2 * WN]
        zrepb = gsb[0:65, :]
        idm = sbt("idmb", [128, 128], BF16)
        idf = sbt("idf", [128, 128])
        sh64b = sbt("sh64b", [64, 128], BF16)
        ones_r = sbt("ones_r", [1, WN], BF16)
        ones1f = sbt("ones1f", [1, 128])
        ones128 = sbt("ones128", [128, 1], BF16)
        rowA = sbt("rowA", [1, S], BF16)
        zrow = rowA[0:1, 0:WN]
        snaprow = rowA[0:1, WN:2 * WN]
        rowB = sbt("rowB", [1, S], BF16)
        rowC = sbt("rowC", [1, S], BF16)
        rowD = sbt("rowD", [1, S], BF16)
        cT = sbt("cT", [8, S], BF16)
        borow = cT[0:1, 0:D]
        drow = rowD
        ctx_f32 = ctxall[:, :].bitcast(F32)
        accbuf = ctx_f32[:, 0:DC * WN]
        sfcT = ctx_f32[0:8, NW * WN:NW * WN + S]
        bvrep = ctx_f32[:, 5 * WN * 2:5 * WN * 2 + D]
        sfctok = ctx_f32[:, 6144:6272]
        ctok = ctxall[:, 12544:12672]
        rscr = sbt("rscr", [128, 26])
        rscr2 = sbt("rscr2", [128, 16])
        bcols = sbt("bcols", [128, 32])
        negc = sbt("negc", [128, 2])
        cstr = sbt("cstr", [1, 16])
        stage = sq[:, :]

        pb = [psb(f"pb{i}", [128, 512]) for i in range(6)]
        pbig = psb("pbig", [128, 1024])

        sems = {k: ctx.enter_context(nc.semaphore(k))
                for k in ("sd", "gd", "ad", "mm", "vv", "aa", "gg")}
        block = ctx.enter_context(nc.Block())

        def xa(c, w=None, cols=None):
            if cols is not None:
                return xA[:, c * S + cols[0]: c * S + cols[1]]
            return xA[:, c * S + w * WN: c * S + (w + 1) * WN]

        def xb(c, w=None, cols=None):
            if cols is not None:
                return xB[:, c * S + cols[0]: c * S + cols[1]]
            return xB[:, c * S + w * WN: c * S + (w + 1) * WN]

        def v65_head(tc, h):
            return v65[:, tc * 1040 + h * 65: tc * 1040 + h * 65 + 65]

        def v65_dst(tc, fw):
            base = tc * 1040 + fw * 520
            return v65[:, base:base + 520].rearrange("p (h f) -> p h f", h=8)[:, :, 0:64]

        def program(p: Prog):
            def dma(q, out, in_, wait=(), slow=False):
                sem = {"sync": "sd", "gpsimd": "gd", "scalar": "ad"}[q]
                return p.do(q, lambda e: e.dma_start(
                    out=out, in_=in_, allow_slow_non_contiguous=slow),
                    wait=wait, inc=(sem, 16))

            def mm(out, lhsT, rhs, start, stop, wait=(), inc=False, tp=None):
                kw = {} if tp is None else {"tile_position": tp}
                return p.do("tensor",
                            lambda e: e.matmul(out, lhsT, rhs, start=start,
                                               stop=stop, **kw),
                            wait=wait, inc=(("mm", 1) if inc else None))

            def vop(fn, wait=(), inc=True):
                return p.do("vector", fn, wait=wait,
                            inc=(("vv", 1) if inc else None))

            def aop(fn, wait=(), inc=True):
                return p.do("scalar", fn, wait=wait,
                            inc=(("aa", 1) if inc else None))

            bank_free = {i: None for i in range(7)}

            def bfree(i):
                t = bank_free[i]
                return (t,) if t is not None else ()

            def bset(i, tok):
                bank_free[i] = tok

            t_id = dma("sync", idf[:], idmi)
            t_idb = dma("gpsimd", idm[:], idmi)
            t_sh = dma("gpsimd", sh64b[:], sh64i)
            vop(lambda v: v.memset(ones_r[:], 1.0))
            vop(lambda v: v.memset(ones1f[:], 1.0))
            vop(lambda v: v.memset(ones128[:], 1.0))
            vop(lambda v: v.memset(
                v65[:].rearrange("p (t h f) -> p t h f", t=NT, h=16)[:, :, :, 64:65],
                1.0))
            t_ones = p.tok("vv")

            t_xc = dma("gpsimd", xbf, x_in)
            for c in range(DC):
                t_x0 = p.do("sync",
                            lambda e, c=c: e.dma_start(
                                out=xA[:, c * S:(c + 1) * S],
                                in_=xbf[:, c * 128:(c + 1) * 128], transpose=True),
                            wait=(t_xc,), inc=("sd", 16))

            t_w0T = None
            if phases >= 2:
                t_cg = dma("gpsimd", wg0bf, wg0)
                t_cu = dma("gpsimd", wu0bf, wu0)
                t_cd = dma("gpsimd", wd0bf, wd0)
                prev = None
                for (src, dst, nblk, tc_, wid) in (
                        (wg0bf, wg0T, DC, t_cg, HID), (wu0bf, wu0T, DC, t_cu, HID),
                        (wd0bf, wd0T, HC0, t_cd, D)):
                    for b in range(nblk):
                        for j in range(wid // 1024):
                            tt = p.do("sync",
                                      lambda e, src=src, b=b, j=j: e.dma_start(
                                          out=stage[:, 0:1024],
                                          in_=src[1024 * j:1024 * (j + 1),
                                                  b * 128:(b + 1) * 128],
                                          transpose=True),
                                      wait=(tc_,) + ((prev,) if prev else ()),
                                      inc=("sd", 16))
                            prev = p.do("sync",
                                        lambda e, dst=dst, b=b, j=j: e.dma_start(
                                            out=dst[b * 128:(b + 1) * 128,
                                                    1024 * j:1024 * (j + 1)],
                                            in_=stage[:, 0:1024]),
                                        wait=(tt,), inc=("sd", 16))
                t_w0T = prev

            def load_slab(buf, w_ap, f0, nf, rows_=D, wait=(), q="gpsimd"):
                rc = rows_ // 128
                src = w_ap.rearrange("(c p) f -> p c f", p=128)[:, :, f0:f0 + nf]
                dst = buf[:, 0:rc * nf].rearrange("p (c f) -> p c f", c=rc)
                return dma(q, dst, src, wait=wait + (p.tok("mm"),))

            def slab_ap(buf, nf, c, cols=None):
                base = c * nf
                if cols is None:
                    return buf[:, base:base + nf]
                return buf[:, base + cols[0]:base + cols[1]]

            def layernorm(src_x, dst_x, dep, apply=True):
                ev_mu = []
                for w in range(NW):
                    tk = None
                    for c in range(DC):
                        tk = mm(pb[4][0:1, :], ones128[:], src_x(c, w),
                                start=(c == 0), stop=(c == DC - 1),
                                wait=(dep + (t_ones,) + bfree(4)) if c == 0 else (),
                                inc=(c == DC - 1))
                    ev = vop(lambda v, w=w: v.tensor_copy(
                        rowA[0:1, w * WN:(w + 1) * WN], pb[4][0:1, :]), wait=(tk,))
                    bset(4, ev)
                    ev_mu.append(ev)
                ev_s2 = []
                for w in range(NW):
                    tk = None
                    for c in range(DC):
                        sqt = sq[:, (c % 2) * WN:(c % 2 + 1) * WN]
                        ta = aop(lambda s, srcap=src_x(c, w), d=sqt: s.activation(
                            d, srcap, AF.Square), wait=(p.tok("mm"),) + dep)
                        tk = mm(pb[4][0:1, :], ones128[:], sqt,
                                start=(c == 0), stop=(c == DC - 1),
                                wait=(ta,) + (bfree(4) if c == 0 else ()),
                                inc=(c == DC - 1))
                    ev = vop(lambda v, w=w: v.tensor_copy(
                        rowB[0:1, w * WN:(w + 1) * WN], pb[4][0:1, :]), wait=(tk,))
                    bset(4, ev)
                    ev_s2.append(ev)
                t1 = vop(lambda v: v.tensor_scalar(rowC[0:1, :], rowA[0:1, :],
                                                   -1.0 / D, None, ALU.mult),
                         wait=tuple(ev_mu))
                t2 = vop(lambda v: v.tensor_tensor(rowD[0:1, :], rowC[0:1, :],
                                                   rowC[0:1, :], ALU.mult), wait=(t1,))
                t3 = vop(lambda v: v.tensor_scalar(rowB[0:1, :], rowB[0:1, :],
                                                   1.0 / D, None, ALU.mult),
                         wait=tuple(ev_s2))
                t4 = vop(lambda v: v.tensor_tensor(rowB[0:1, :], rowB[0:1, :],
                                                   rowD[0:1, :], ALU.subtract),
                         wait=(t2, t3))
                t5 = vop(lambda v: v.tensor_scalar(rowB[0:1, :], rowB[0:1, :],
                                                   EPS, None, ALU.add), wait=(t4,))
                t6 = vop(lambda v: v.reciprocal(rowD[0:1, :], rowB[0:1, :]),
                         wait=(t5,))
                t7 = aop(lambda s: s.activation(rowA[0:1, :], rowD[0:1, :], AF.Sqrt),
                         wait=(t6,))
                t8 = vop(lambda v: v.tensor_tensor(rowB[0:1, :], rowC[0:1, :],
                                                   rowA[0:1, :], ALU.mult), wait=(t7,))
                if not apply:
                    return t8
                last = None
                for w in range(NW):
                    tr = mm(pb[4][:, :], ones_r[0:1, 0:128],
                            rowA[0:1, w * WN:(w + 1) * WN],
                            start=True, stop=True, wait=(t7,) + bfree(4), inc=True)
                    tbm = mm(pb[5][:, :], ones_r[0:1, 0:128],
                             rowB[0:1, w * WN:(w + 1) * WN],
                             start=True, stop=True, wait=(t8,) + bfree(5), inc=True)
                    e1 = vop(lambda v: v.tensor_copy(reprow[:, 0:WN], pb[4][:, :]),
                             wait=(tr,))
                    e2 = vop(lambda v: v.tensor_copy(reprow[:, WN:2 * WN], pb[5][:, :]),
                             wait=(tbm,))
                    bset(4, e1)
                    bset(5, e2)
                    for c in range(DC):
                        vop(lambda v, c=c, w=w: v.tensor_tensor(
                            dst_x(c, w), src_x(c, w), reprow[:, 0:WN], ALU.mult),
                            wait=(e1,) + dep)
                        last = vop(lambda v, c=c, w=w: v.tensor_tensor(
                            dst_x(c, w), dst_x(c, w), reprow[:, WN:2 * WN], ALU.add),
                            wait=(e2,))
                return last

            def attention(layer, dep=()):
                src = xa
                gdep = (p.tok("aa"), p.tok("mm"), p.tok("vv"))
                if layer == 0:
                    tbq = dma("sync", bcols[:, 0:8],
                              bq_i.rearrange("(c p) -> p c", p=128), slow=True,
                              wait=gdep)
                    tbl = dma("sync", bcols[:, 8:10],
                              bl_i.rearrange("(c p) -> p c", p=128), slow=True,
                              wait=gdep)
                    tbv = dma("gpsimd", rowA[0:1, 0:D],
                              bv_i.rearrange("(a b) -> a b", a=1), wait=gdep)
                    tbo = dma("gpsimd", rowC[0:1, 0:D],
                              bo_i.rearrange("(a b) -> a b", a=1))
                else:
                    tbq = dma("sync", bcols[:, 0:8],
                              AP(bqkv.tensor, 0, [[1, 128], [128, 8]]), slow=True,
                              wait=gdep)
                    tbl = tbq
                    tbv = dma("gpsimd", rowA[0:1, 0:D],
                              AP(bqkv.tensor, 2 * D, [[1, 1], [1, D]]), wait=gdep)
                    tbo = dma("gpsimd", rowC[0:1, 0:D],
                              bout_i.rearrange("(a b) -> a b", a=1))
                tqs = vop(lambda v: v.tensor_scalar(bcols[:, 0:8], bcols[:, 0:8],
                                                    QS, None, ALU.mult), wait=(tbq,))
                t_bvrep = None
                for half in range(2):
                    tr = mm(pb[4][:, :], ones_r[0:1, 0:128],
                            rowA[0:1, half * 512:(half + 1) * 512],
                            start=True, stop=True,
                            wait=(tbv, t_ones) + bfree(4), inc=True)
                    t_bvrep = vop(lambda v, half=half: v.tensor_copy(
                        bvrep[:, half * 512:(half + 1) * 512], pb[4][:, :]),
                        wait=(tr, p.tok("mm")))
                    bset(4, t_bvrep)
                tbor = vop(lambda v: v.tensor_copy(borow[0:1, :], rowC[0:1, 0:D]),
                           wait=(tbo,))
                if layer == 1:
                    tev0 = dma("gpsimd", sq[0:1, 960:1024], relv[0:1, :],
                               wait=(p.tok("vv"), p.tok("mm")))
                    tr0 = mm(pb[4][:, 0:64], ones_r[0:1, 0:128], sq[0:1, 960:1024],
                             start=True, stop=True,
                             wait=(tev0, t_ones) + bfree(4), inc=True)
                    t_bvrep = vop(lambda v: v.tensor_tensor(
                        bvrep[:, :].rearrange("p (h f) -> p h f", h=16),
                        bvrep[:, :].rearrange("p (h f) -> p h f", h=16),
                        pb[4][:, 0:64].unsqueeze(1).broadcast_to([128, 16, 64]),
                        ALU.add), wait=(tr0, t_bvrep))
                    bset(4, t_bvrep)

                if layer == 0:
                    tws = load_slab(wmA, Wl, 0, LAT)
                    t_lat = None
                    for w in range(NW):
                        for fc in range(LC):
                            tk = None
                            for c in range(DC):
                                tk = mm(pb[0][:, :],
                                        slab_ap(wmA, LAT, c, (fc * 128, fc * 128 + 128)),
                                        src(c, w), start=(c == 0), stop=(c == DC - 1),
                                        wait=((tws,) + dep + bfree(0)) if c == 0 else (),
                                        inc=(c == DC - 1))
                            t_lat = aop(lambda s, fc=fc, w=w: s.activation(
                                latb[:, fc * S + w * WN: fc * S + (w + 1) * WN],
                                pb[0][:, :], AF.Identity,
                                bias=bcols[:, 8 + fc:9 + fc], scale=1.0),
                                wait=(tk, tbl))
                            bset(0, t_lat)
                    twv = load_slab(wmB, Wv, 0, D, rows_=LAT)
                    t_v = None
                    for tc in range(NT):
                        for fw in range(2):
                            tk = None
                            for lc in range(LC):
                                tk = mm(pb[0][:, :],
                                        latb[:, lc * S + tc * 128:lc * S + (tc + 1) * 128],
                                        wmB[:, lc * D + fw * WN: lc * D + (fw + 1) * WN],
                                        start=(lc == 0), stop=(lc == LC - 1),
                                        wait=((twv, p.tok("aa")) + bfree(0))
                                        if lc == 0 else (),
                                        inc=(lc == LC - 1))
                            t_v = vop(lambda v, tc=tc, fw=fw: v.tensor_tensor(
                                v65_dst(tc, fw),
                                pb[0][:, :].rearrange("p (h f) -> p h f", h=8),
                                bvrep[:, fw * WN:(fw + 1) * WN].rearrange(
                                    "p (h f) -> p h f", h=8),
                                ALU.add), wait=(tk, t_bvrep, t_ones))
                            bset(0, t_v)
                else:
                    twv = load_slab(wmA, Wqkv, 2 * D, WN)
                    twv2 = load_slab(wmB, Wqkv, 2 * D + WN, WN)
                    t_v = None
                    for tc in range(NT):
                        for fw in range(2):
                            buf, tw = (wmA, twv) if fw == 0 else (wmB, twv2)
                            tk = None
                            for c in range(DC):
                                tk = mm(pb[0][:, :],
                                        src(c, None, cols=(tc * 128, (tc + 1) * 128)),
                                        buf[:, c * WN:(c + 1) * WN],
                                        start=(c == 0), stop=(c == DC - 1),
                                        wait=((tw,) + dep + bfree(0)) if c == 0 else (),
                                        inc=(c == DC - 1))
                            t_v = vop(lambda v, tc=tc, fw=fw: v.tensor_tensor(
                                v65_dst(tc, fw),
                                pb[0][:, :].rearrange("p (h f) -> p h f", h=8),
                                bvrep[:, fw * WN:(fw + 1) * WN].rearrange(
                                    "p (h f) -> p h f", h=8),
                                ALU.add), wait=(tk, t_bvrep, t_ones))
                            bset(0, t_v)

                t_eext = None
                vop(lambda v: v.memset(cstr[0:1, 0:1], 0.0))
                if layer == 1:
                    te1 = dma("sync", outt[:, 0:64], relk[0:128, :],
                              wait=(p.tok("vv"), p.tok("aa"), p.tok("mm")))
                    te2 = dma("sync", outt[0:1, 64:128], relk[128:129, :])
                    tr = mm(pb[4][:, 0:64], ones1f[:], outt[0:1, 0:64],
                            start=True, stop=True, wait=(te1, t_ones) + bfree(4),
                            inc=True)
                    tsub = vop(lambda v: v.tensor_tensor(
                        sq[:, 0:64], outt[:, 0:64], pb[4][:, 0:64], ALU.subtract),
                        wait=(tr,))
                    bset(4, tsub)
                    tsub2 = vop(lambda v: v.tensor_tensor(
                        sq[0:1, 64:128], outt[0:1, 64:128], outt[0:1, 0:64],
                        ALU.subtract), wait=(te2, te1))
                    tz = vop(lambda v: v.memset(eext[:, :], 0.0))
                    tt1 = mm(pb[4][0:64, 0:128], sq[:, 0:64], idm[:],
                             start=True, stop=False,
                             wait=(tsub, t_idb) + bfree(4))
                    tt1 = mm(pb[4][64:128, 0:128], sq[:, 0:64], idm[:],
                             start=False, stop=True, tp=(0, 64), inc=True)
                    tev = vop(lambda v: v.tensor_copy(eext[:, 192:320],
                                                      pb[4][:, 0:128]),
                              wait=(tt1, tz))
                    bset(4, tev)
                    tt2 = mm(pb[4][0:64, 0:1], sq[0:1, 64:128], ones_r[0:1, 0:1],
                             start=True, stop=False,
                             wait=(tsub2, t_ones) + bfree(4))
                    tt2 = mm(pb[4][64:128, 0:1], sq[0:1, 64:128], ones_r[0:1, 0:1],
                             start=False, stop=True, tp=(0, 64), inc=True)
                    tev2 = vop(lambda v: v.tensor_copy(eext[:, 320:321],
                                                       pb[4][:, 0:1]),
                               wait=(tt2, tev))
                    bset(4, tev2)
                    tev3 = vop(lambda v: v.tensor_copy(
                        eext[:, 321:512], eext[:, 320:321].broadcast_to([128, 191])),
                        wait=(tev2,))
                    trep = vop(lambda v: v.tensor_copy(
                        e128rep[:, :], eext[:, 320:321].broadcast_to([128, 128])),
                        wait=(tev2,))
                    tmx = vop(lambda v: v.reduce_max(rscr[:, 0:1], sq[:, 0:64],
                                                     axis=AX.X,
                                                     apply_absolute_value=True),
                              wait=(tsub,))
                    tmx2 = vop(lambda v: v.reduce_max(rscr[0:1, 1:2], sq[0:1, 64:128],
                                                      axis=AX.X,
                                                      apply_absolute_value=True),
                               wait=(tsub2,))
                    tt3 = mm(pb[4][0:1, 0:128], rscr[:, 0:1], idf[:],
                             start=True, stop=True, wait=(tmx, t_id) + bfree(4),
                             inc=True)
                    tm4 = vop(lambda v: v.reduce_max(cstr[0:1, 0:1],
                                                     pb[4][0:1, 0:128], axis=AX.X),
                              wait=(tt3,))
                    bset(4, tm4)
                    tm5 = vop(lambda v: v.tensor_tensor(cstr[0:1, 0:1], cstr[0:1, 0:1],
                                                        rscr[0:1, 1:2], ALU.max),
                              wait=(tm4, tmx2))
                    t_eext = vop(lambda v: v.tensor_scalar(
                        cstr[0:1, 0:1], cstr[0:1, 0:1], 8.0, None, ALU.mult),
                        wait=(tm5, tev, tev3, trep))
                    tv1 = dma("sync", outt[:, 64:128], relv[0:128, :],
                              wait=(p.tok("mm"), p.tok("vv")))
                    tv2 = dma("sync", outt[0:1, 128:192], relv[128:129, :])
                    tv3 = vop(lambda v: v.tensor_copy(sq[:, 128:192], outt[:, 64:128]),
                              wait=(tv1,))
                    tv4 = vop(lambda v: v.tensor_tensor(
                        ev128d[0:1, :], outt[0:1, 128:192], outt[0:1, 64:128],
                        ALU.subtract), wait=(tv2, tv1))
                    tv2a = vop(lambda v: v.tensor_copy(sq[0:1, 192:256],
                                                       outt[0:1, 128:192]),
                               wait=(tv2,))
                    tv5 = dma("gpsimd", selb[:, :], seli[0:128, :], wait=(p.tok("mm"),))
                    tv6 = dma("gpsimd", selr[0:1, :], seli[128:129, :], wait=(p.tok("mm"),))
                    tevd = None
                    for cc3 in range(3):
                        m1 = mm(pb[5][0:128, 0:64], selb[:, cc3 * 128:(cc3 + 1) * 128],
                                sq[:, 128:192], start=True, stop=False,
                                wait=(tv3, tv5) + bfree(5))
                        m2 = mm(pb[5][0:128, 0:64],
                                selr[0:1, cc3 * 128:(cc3 + 1) * 128],
                                sq[0:1, 192:256],
                                start=False, stop=True, wait=(tv6, tv2a), inc=True)
                        tevd = vop(lambda v, cc3=cc3: v.tensor_copy(
                            evd[:, cc3 * 64:(cc3 + 1) * 64], pb[5][0:128, 0:64]),
                            wait=(m2,))
                        bset(5, tevd)
                    tz2 = vop(lambda v: v.memset(outt[:, 0:640], 0.0),
                              wait=(tv3, tv4, tv2a))
                    tz_toks = []
                    for sl2 in range(NSLOT):
                        tz_toks.append(dma("sync",
                                           AP(skew2.tensor, sl2 * 128 * 640,
                                              [[640, 128], [1, 640]]),
                                           outt[:, 0:640].bitcast(BF16)[:, 0:640],
                                           wait=(tz2,)))
                    tzslot = tz_toks[-1]
                    p.__dict__["slot2_init"] = tz_toks
                    t_eext = vop(lambda v: v.tensor_copy(cstr[0:1, 1:2],
                                                         cstr[0:1, 0:1]),
                                 wait=(t_eext, tevd, tzslot, tv4))

                WQ = Wq if layer == 0 else Wqkv

                def q_slab(pair, buf):
                    return load_slab(buf, WQ, pair * 128, 128)

                def k_slab(pair, buf):
                    if layer == 0:
                        return load_slab(buf, Wk, pair * 128, 128, rows_=LAT)
                    return load_slab(buf, Wqkv, D + pair * 128, 128)

                tq_s = [None] * NP
                tk_s = [None] * NP
                tq_s[0] = q_slab(0, wsA)
                tk_s[0] = k_slab(0, wsB)
                t_qp_free = ()
                t_kp_free = ()
                t_att_free = [None, None]
                slot_free = [None] * NSLOT
                slot2_free = list(p.__dict__.get("slot2_init", [None] * NSLOT)) \
                    if layer == 1 else [None] * NSLOT
                bwin_free = [None] * 4
                pext_free = [None]
                slot_i = [0]
                t_ctx_evac = []

                for pair in range(NP):
                    qbuf = wsA if pair % 2 == 0 else wsC
                    kbuf = wsB if pair % 2 == 0 else wsD
                    if pair + 1 < NP:
                        nqb = wsC if pair % 2 == 0 else wsA
                        nkb = wsD if pair % 2 == 0 else wsB
                        tq_s[pair + 1] = q_slab(pair + 1, nqb)
                        tk_s[pair + 1] = k_slab(pair + 1, nkb)

                    tq_done = None
                    for w in range(NW):
                        tk = None
                        for c in range(DC):
                            tk = mm(pb[0][:, :], slab_ap(qbuf, 128, c), src(c, w),
                                    start=(c == 0), stop=(c == DC - 1),
                                    wait=((tq_s[pair],) + dep + bfree(0))
                                    if c == 0 else (),
                                    inc=(c == DC - 1))
                        tq_done = aop(lambda s, w=w: s.activation(
                            qp[:, w * WN:(w + 1) * WN], pb[0][:, :], AF.Identity,
                            bias=bcols[:, pair:pair + 1], scale=QS),
                            wait=(tk, tqs) + t_qp_free)
                        bset(0, tq_done)
                    tk_done = None
                    nsrc = LC if layer == 0 else DC
                    for w in range(NW):
                        tk = None
                        for c in range(nsrc):
                            msrc = (latb[:, c * S + w * WN: c * S + (w + 1) * WN]
                                    if layer == 0 else src(c, w))
                            tk = mm(pb[1][:, :], slab_ap(kbuf, 128, c), msrc,
                                    start=(c == 0), stop=(c == nsrc - 1),
                                    wait=((tk_s[pair], p.tok("aa")) + bfree(1))
                                    if c == 0 else (),
                                    inc=(c == nsrc - 1))
                        tk_done = vop(lambda v, w=w: v.tensor_copy(
                            kp[:, w * WN:(w + 1) * WN], pb[1][:, :]),
                            wait=(tk,) + t_kp_free)
                        bset(1, tk_done)

                    tm1 = vop(lambda v, pair=pair: v.scalar_tensor_tensor(
                        ctxall[:, pair * S:(pair + 1) * S], qp[:, :], 1.0, qp[:, :],
                        ALU.mult, ALU.mult, accum_out=rscr[:, 0:1]),
                        wait=(tq_done, p.tok("aa"), p.tok("mm")))
                    tm2 = vop(lambda v, pair=pair: v.scalar_tensor_tensor(
                        ctxall[:, pair * S:(pair + 1) * S], kp[:, :], 1.0, kp[:, :],
                        ALU.mult, ALU.mult, accum_out=rscr[:, 1:2]),
                        wait=(tk_done,))
                    mm(pb[4][0:1, 0:128], rscr[:, 0:1], idf[:], start=True,
                       stop=False, wait=(tm1, tm2, t_id) + bfree(4))
                    ttr = mm(pb[4][0:1, 128:256], rscr[:, 1:2], idf[:], start=False,
                             stop=True, inc=True)
                    tred = []
                    for j in range(4):
                        tred.append(vop(lambda v, j=j: v.reduce_sum(
                            cstr[0:1, 2 + j:3 + j], pb[4][0:1, 64 * j:64 * j + 64],
                            axis=AX.X), wait=(ttr,)))
                    bset(4, tred[-1])
                    tln = aop(lambda sc: sc.activation(cstr[0:1, 6:10],
                                                       cstr[0:1, 2:6], AF.Ln),
                              wait=tuple(tred))
                    tad = vop(lambda v: v.tensor_scalar(
                        cstr[0:1, 6:10], cstr[0:1, 6:10],
                        float(np.log(3.0 / 2048.0)), None, ALU.add), wait=(tln,))
                    texp = aop(lambda sc: sc.activation(
                        rscr[0:1, 4:8], cstr[0:1, 6:10], AF.Exp, scale=0.5),
                        wait=(tad,))
                    tce = vop(lambda v: v.tensor_scalar(
                        rscr[0:1, 6:8], rscr[0:1, 6:8], cstr[0:1, 0:1], None,
                        ALU.add), wait=(texp,) + ((t_eext,) if layer == 1 else ()))
                    tcm = vop(lambda v: v.tensor_tensor(
                        rscr[0:1, 8:10], rscr[0:1, 4:6], rscr[0:1, 6:8], ALU.mult),
                        wait=(tce,))
                    tneg = vop(lambda v: v.tensor_scalar(
                        rscr[0:1, 10:12], rscr[0:1, 8:10], -1.0, None, ALU.mult),
                        wait=(tcm,))
                    t_negc = None
                    for hh in range(2):
                        trr = mm(pb[5][:, 0:1], ones1f[:], rscr[0:1, 10 + hh:11 + hh],
                                 start=True, stop=True, wait=(tneg,) + bfree(5),
                                 inc=True)
                        t_negc = vop(lambda v, hh=hh: v.tensor_copy(
                            negc[:, hh:hh + 1], pb[5][:, 0:1]), wait=(trr,))
                        bset(5, t_negc)

                    t_tmpE = [None] * NW
                    for hh in range(2):
                        h_abs = 2 * pair + hh
                        sbank = hh
                        cbank = 2 + hh
                        t_dr = None
                        if layer == 1:
                            for w in range(NW):
                                td = mm(pb[5][:, :],
                                        e128rep[64 * hh:64 * hh + 64, :],
                                        qp[64 * hh:64 * hh + 64, w * WN:(w + 1) * WN],
                                        start=True, stop=True,
                                        wait=(t_eext, tq_done) + bfree(5), inc=True,
                                        tp=(64 * hh, 0))
                                t_dr = vop(lambda v, w=w: v.tensor_copy(
                                    drow[0:1, w * WN:(w + 1) * WN], pb[5][0:1, :]),
                                    wait=(td, p.tok("mm")))
                                bset(5, t_dr)
                        for w in range(NW):
                            tband = {}
                            t_atx = [None] * 4
                            t_snap = [None] * 4
                            t_snap_gate = [None]
                            if layer == 1:
                                for ti in range(4):
                                    tcg = 4 * w + ti
                                    tpm = mm(pb[5][:, :],
                                             qp[64 * hh:64 * hh + 64,
                                                tcg * 128:(tcg + 1) * 128],
                                             eext[64 * hh:64 * hh + 64, :],
                                             start=True, stop=True,
                                             wait=(t_eext, tq_done) + bfree(5),
                                             inc=True, tp=(64 * hh, 0))
                                    pxt = pext[:, :]
                                    tpe = vop(lambda v, pxt=pxt: v.tensor_copy(
                                        pxt, pb[5][:, :]),
                                        wait=(tpm,) +
                                        ((pext_free[0],)
                                         if pext_free[0] else ()))
                                    bset(5, tpe)
                                    sl = slot_i[0] % NSLOT
                                    slot_i[0] += 1
                                    tw1 = dma("sync",
                                              AP(skew.tensor, sl * 128 * 512,
                                                 [[512, 128], [1, 512]]),
                                              pxt,
                                              wait=(tpe,) +
                                              ((slot_free[sl],)
                                               if slot_free[sl] else ()))
                                    pext_free[0] = tw1
                                    tw2 = dma("sync",
                                              bwin[:, ti * 384:(ti + 1) * 384],
                                              AP(skew.tensor, sl * 128 * 512 + 128,
                                                 [[511, 128], [1, 384]]),
                                              wait=(tw1,) +
                                              ((bwin_free[ti],)
                                               if bwin_free[ti] else ()))
                                    slot_free[sl] = tw2
                                    tband[tcg] = tw2
                                    t = tcg
                                    qb = pb[1 - hh]
                                    tb = pb[3 - hh]
                                    m_lo = max(0, t - 1)
                                    m_hi = min(NT, t + 2)
                                    kcol = (m_lo * 128, m_hi * 128)
                                    pcol = ((m_lo - (t - 1)) * 128,
                                            (m_hi - (t - 1)) * 128)
                                    tqw = mm(qb[:, pcol[0]:pcol[1]],
                                             qp[64 * hh:64 * hh + 64,
                                                t * 128:(t + 1) * 128],
                                             kp[64 * hh:64 * hh + 64,
                                                kcol[0]:kcol[1]],
                                             start=True, stop=False,
                                             wait=(tk_done, tq_done) +
                                             bfree(1 - hh), tp=(64 * hh, 0))
                                    tqw = mm(qb[:, pcol[0]:pcol[1]], idm[:],
                                             bwin[:, ti * 384 + pcol[0]:
                                                  ti * 384 + pcol[1]],
                                             start=False, stop=True,
                                             wait=(tw2, t_idb), inc=True)
                                    texq = aop(lambda sc, qb=qb, pc=pcol:
                                               sc.activation(
                                                   qwin[:, pc[0]:pc[1]],
                                                   qb[:, pc[0]:pc[1]], AF.Exp,
                                                   bias=negc[:, hh:hh + 1]),
                                               wait=(tqw, t_negc, p.tok("sd")))
                                    bset(1 - hh, texq)
                                    if pcol[0] > 0:
                                        texq = vop(lambda v, pc=pcol: v.memset(
                                            qwin[:, 0:pc[0]], 0.0), wait=(texq,))
                                    if pcol[1] < 384:
                                        texq = vop(lambda v, pc=pcol: v.memset(
                                            qwin[:, pc[1]:384], 0.0), wait=(texq,))
                                    sl2 = (slot_i[0] + 3) % NSLOT
                                    tw3 = dma("sync",
                                              AP(skew2.tensor, sl2 * 128 * 640 + 128,
                                                 [[640, 128], [1, 384]]),
                                              qwin[:, :],
                                              wait=(texq,) +
                                              ((slot2_free[sl2],)
                                               if slot2_free[sl2] else ()))
                                    tw4 = dma("sync", sq[:, 0:384],
                                              AP(skew2.tensor, sl2 * 128 * 640 + 128,
                                                 [[641, 128], [1, 384]]),
                                              wait=(tw3, p.tok("mm"), p.tok("vv")))
                                    slot2_free[sl2] = tw4
                                    tt4 = None
                                    for jc in range(3):
                                        tt4 = mm(tb[:, jc * 128:(jc + 1) * 128],
                                                 sq[:, jc * 128:(jc + 1) * 128],
                                                 idm[:], start=(jc == 0),
                                                 stop=(jc == 2),
                                                 wait=((tw4, t_idb) + bfree(3 - hh))
                                                 if jc == 0 else (),
                                                 inc=(jc == 2))
                                    tae = vop(lambda v, ti=ti, tb=tb: v.tensor_copy(
                                        atx[:, ti * 384:(ti + 1) * 384], tb[:, 0:384]),
                                        wait=(tt4,))
                                    bset(3 - hh, tae)
                                    t_atx[ti] = tae
                            for m in range(NT):
                                bank = pb[sbank]
                                tp = (0, 0) if hh == 0 else (64, 0)
                                extras = []
                                if layer == 1:
                                    for tcg in (m - 1, m, m + 1):
                                        if 4 * w <= tcg < 4 * w + 4:
                                            extras.append(("band", tcg))
                                    qhi = min(128 * (m - 1), (w + 1) * WN) - w * WN
                                    if qhi > 0:
                                        extras.append(("far", qhi))
                                n_ops = 1 + len(extras)
                                ti_mm = mm(bank[:, :],
                                           kp[64 * hh:64 * hh + 64,
                                              m * 128:(m + 1) * 128],
                                           qp[64 * hh:64 * hh + 64,
                                              w * WN:(w + 1) * WN],
                                           start=True, stop=(n_ops == 1),
                                           wait=(tk_done, tq_done, t_negc) +
                                           bfree(sbank),
                                           inc=(n_ops == 1), tp=tp)
                                for i, (kind, arg) in enumerate(extras):
                                    lastop = (i == len(extras) - 1)
                                    if kind == "band":
                                        tcg = arg
                                        ti = tcg - 4 * w
                                        blk = m - tcg + 1
                                        ti_mm = mm(
                                            bank[:, (tcg - 4 * w) * 128:
                                                 (tcg - 4 * w + 1) * 128],
                                            bwin[:, ti * 384 + blk * 128:
                                                 ti * 384 + (blk + 1) * 128],
                                            idm[:], start=False, stop=lastop,
                                            wait=(tband[tcg], t_idb), inc=lastop)
                                    else:
                                        qhi = arg
                                        ti_mm = mm(bank[:, 0:qhi], ones_r[0:1, 0:128],
                                                   drow[0:1, w * WN:w * WN + qhi],
                                                   start=False, stop=lastop,
                                                   wait=(t_dr, t_ones), inc=lastop)
                                att = attnT[:, (m % 2) * WN:(m % 2 + 1) * WN]
                                texp = aop(lambda s, bank=bank, att=att, hh=hh:
                                           s.activation(att, bank[:, :], AF.Exp,
                                                        bias=negc[:, hh:hh + 1]),
                                           wait=(ti_mm, t_negc) +
                                           ((t_att_free[m % 2],)
                                            if t_att_free[m % 2] else ()))
                                bset(sbank, texp)
                                last_ctx = (m == NT - 1) and (layer == 0)
                                tctx = mm(pb[cbank][0:65, :], v65_head(m, h_abs),
                                          att, start=(m == 0), stop=last_ctx,
                                          wait=(texp, t_v, t_snap_gate[0]) +
                                          (bfree(cbank) if m == 0 else ()),
                                          inc=True)
                                t_att_free[m % 2] = tctx
                                if layer == 1 and (m - 1) in (4 * w, 4 * w + 1,
                                                              4 * w + 2, 4 * w + 3) \
                                        and m <= NT - 1 and (m - 1) <= NT - 3:
                                    ti2 = m - 1 - 4 * w
                                    tsnp = vop(lambda v, ti2=ti2, cb=pb[cbank]:
                                               v.tensor_copy(
                                        snaprow[0:1, ti2 * 128:(ti2 + 1) * 128],
                                        cb[64:65, ti2 * 128:(ti2 + 1) * 128]),
                                        wait=(tctx,))
                                    t_snap[ti2] = tsnp
                                    t_snap_gate[0] = tsnp
                            if layer == 1:
                                tzr = vop(lambda v, cb=pb[cbank]: v.tensor_copy(
                                    zrow[0:1, :], cb[64:65, :]),
                                    wait=(p.tok("mm"),))
                                specs = []
                                tfr4 = {}
                                for ti in range(4):
                                    t = 4 * w + ti
                                    for jc in range(3):
                                        specs.append(("pev", ti, jc))
                                    if t <= NT - 3:
                                        tfr4[ti] = vop(
                                            lambda v, ti=ti: v.tensor_tensor(
                                                snaprow[0:1, ti * 128:(ti + 1) * 128],
                                                zrow[0:1, ti * 128:(ti + 1) * 128],
                                                snaprow[0:1,
                                                        ti * 128:(ti + 1) * 128],
                                                ALU.subtract),
                                            wait=(t_snap[ti], tzr))
                                        specs.append(("far", ti, 0))
                                for si, (kind, ti, jc) in enumerate(specs):
                                    lastop = (si == len(specs) - 1)
                                    if kind == "pev":
                                        mm(pb[cbank][0:64, ti * 128:(ti + 1) * 128],
                                           evd[:, jc * 64:(jc + 1) * 64],
                                           atx[:, ti * 384 + jc * 128:
                                               ti * 384 + (jc + 1) * 128],
                                           start=False, stop=lastop,
                                           wait=(t_atx[ti], t_eext, tzr),
                                           inc=lastop)
                                    else:
                                        mm(pb[cbank][0:64, ti * 128:(ti + 1) * 128],
                                           ev128d[0:1, :],
                                           snaprow[0:1, ti * 128:(ti + 1) * 128],
                                           start=False, stop=lastop,
                                           wait=(tfr4[ti],), inc=lastop)
                                for ti in range(4):
                                    bwin_free[ti] = p.tok("mm")
                            trc = vop(lambda v, cb=pb[cbank]: v.reciprocal(
                                rowC[0:1, 0:WN], cb[64:65, :]),
                                wait=(p.tok("mm"),))
                            trz = mm(pb[4][0:65, :], ones_r[0:1, 0:65],
                                     rowC[0:1, 0:WN],
                                     start=True, stop=True,
                                     wait=(trc,) + bfree(4), inc=True)
                            tez = vop(lambda v: v.tensor_copy(zrepb[:, :],
                                                              pb[4][0:65, :]),
                                      wait=(trz,))
                            bset(4, tez)
                            if hh == 0:
                                t_tmpE[w] = vop(
                                    lambda v, cb=pb[cbank], w=w, pair=pair:
                                    v.tensor_tensor(
                                        ctxall[0:64, pair * S + w * WN:
                                               pair * S + (w + 1) * WN],
                                        cb[0:64, :], zrepb[0:64, :], ALU.mult),
                                    wait=(tez, p.tok("mm")))
                                bset(cbank, t_tmpE[w])
                            else:
                                tno = vop(lambda v, cb=pb[cbank]: v.tensor_tensor(
                                    tmpO[:, :], cb[0:64, :], zrepb[0:64, :],
                                    ALU.mult), wait=(tez, p.tok("mm")))
                                bset(cbank, tno)
                                tc1 = mm(pb[5][:, :], idm[0:64, :],
                                         ctxall[0:64, pair * S + w * WN:
                                                pair * S + (w + 1) * WN],
                                         start=True, stop=False,
                                         wait=(t_tmpE[w], t_idb) + bfree(5))
                                tc2 = mm(pb[5][:, :], sh64b[:], tmpO[:, :],
                                         start=False, stop=True,
                                         wait=(tno, t_sh), inc=True)
                                tce = vop(lambda v, pair=pair, w=w: v.tensor_copy(
                                    ctxall[:, pair * S + w * WN:
                                           pair * S + (w + 1) * WN],
                                    pb[5][:, :]), wait=(tc2,))
                                bset(5, tce)
                                t_ctx_evac.append(tce)
                    t_qp_free = (p.tok("mm"), p.tok("vv"))
                    t_kp_free = (p.tok("mm"),)

                WO = Wo if layer == 0 else Wout
                two = [None] * DC
                two[0] = load_slab(wsA, WO, 0, 128)
                tlast = None
                for fc in range(DC):
                    if fc + 1 < DC:
                        two[fc + 1] = load_slab(wsB if fc % 2 == 0 else wsA, WO,
                                                (fc + 1) * 128, 128)
                    buf = wsA if fc % 2 == 0 else wsB
                    for w in range(NW):
                        bi = (fc * NW + w) % 2
                        bank = pb[bi]
                        for cc in range(DC):
                            mm(bank[:, :], slab_ap(buf, 128, cc),
                               ctxall[:, cc * S + w * WN: cc * S + (w + 1) * WN],
                               start=(cc == 0), stop=False,
                               wait=((two[fc],) + tuple(t_ctx_evac[-8:]) +
                                     bfree(bi)) if cc == 0 else ())
                        mm(bank[:, :], borow[0:1, fc * 128:(fc + 1) * 128],
                           ones_r[0:1, :], start=False, stop=False,
                           wait=(tbor, t_ones))
                        tk = mm(bank[:, :], idm[:], src(fc, w), start=False,
                                stop=True, wait=(t_idb,), inc=True)
                        tev = vop(lambda v, fc=fc, w=w, bank=bank: v.tensor_copy(
                            xb(fc, w), bank[:, :]), wait=(tk,))
                        bset(bi, tev)
                        tlast = tev
                return tlast

            t_attn0 = attention(0, dep=(t_x0,))
            t_ln = layernorm(xb, xa, (t_attn0, p.tok("mm"), t_w0T))
            if debug:
                dma("gpsimd", dbg["dx1"], xA[:], wait=(t_ln, p.tok("mm"), p.tok("aa")))

            if phases >= 2:
                t_h_free = None
                t_gsb_free = None
                t_ffn = None
                tg_s = load_slab(wsA, wg0T, 0, 128, q="sync", wait=(t_w0T,))
                tu_s = load_slab(wsB, wu0T, 0, 128, q="sync", wait=(t_w0T,))
                for w in range(NW):
                    tg_cur, tu_cur = tg_s, tu_s
                    for hc in range(HC0):
                        gbuf = wsA if hc % 2 == 0 else wsC
                        ubuf = wsB if hc % 2 == 0 else wsD
                        nxt = hc + 1 if hc + 1 < HC0 else (0 if w + 1 < NW else None)
                        if nxt is not None:
                            tg_s = load_slab(wsC if hc % 2 == 0 else wsA, wg0T,
                                             nxt * 128, 128, q="sync",
                                             wait=(t_w0T,))
                            tu_s = load_slab(wsD if hc % 2 == 0 else wsB, wu0T,
                                             nxt * 128, 128, q="sync",
                                             wait=(t_w0T,))
                        tk = None
                        for c in range(DC):
                            tk = mm(pb[0][:, :], slab_ap(gbuf, 128, c), xa(c, w),
                                    start=(c == 0), stop=(c == DC - 1),
                                    wait=((tg_cur, t_ln) + bfree(0)) if c == 0 else (),
                                    inc=(c == DC - 1))
                        tku = None
                        for c in range(DC):
                            tku = mm(pb[1][:, :], slab_ap(ubuf, 128, c), xa(c, w),
                                     start=(c == 0), stop=(c == DC - 1),
                                     wait=((tu_cur, t_ln) + bfree(1)) if c == 0 else (),
                                     inc=(c == DC - 1))
                        tg_cur, tu_cur = tg_s, tu_s
                        tsl = aop(lambda sc, hc=hc: sc.activation(
                            gsb[:, :], pb[0][:, :], AF.Silu, scale=DEQ),
                            wait=(tk,) + bfree(0) +
                            ((t_gsb_free,) if t_gsb_free else ()))
                        bset(0, tsl)
                        th = vop(lambda v, hc=hc: v.scalar_tensor_tensor(
                            hbuf[:, hc * WN:(hc + 1) * WN], pb[1][:, :],
                            DEQ * DEQ, gsb[:, :], ALU.mult, ALU.mult),
                            wait=(tku, tsl) +
                            ((t_h_free,) if t_h_free else ()))
                        bset(1, th)
                        t_gsb_free = th
                    def dn_bank(dc):
                        if dc < 6:
                            return pb[dc][:, :]
                        return pbig[:, (dc - 6) * 512:(dc - 5) * 512]

                    trow = [None, None]
                    t_hcdone = [None] * HC0
                    trow[0] = dma("sync", wsC[:, 0:1024],
                                  wd0T.rearrange("(hc p) d -> p hc d", p=128)[:, 0, :],
                                  wait=(p.tok("mm"), t_w0T))
                    for hc in range(HC0):
                        buf = wsC if hc % 2 == 0 else wsD
                        if hc + 1 < HC0:
                            trow[(hc + 1) % 2] = dma(
                                "sync", (wsD if hc % 2 == 0 else wsC)[:, 0:1024],
                                wd0T.rearrange("(hc p) d -> p hc d", p=128)[:, hc + 1, :],
                                wait=(p.tok("mm"), t_w0T) +
                                ((t_hcdone[hc - 1],) if hc >= 1 else ()))
                        for dc in range(DC):
                            tk2 = mm(dn_bank(dc), buf[:, dc * 128:(dc + 1) * 128],
                                     hbuf[:, hc * WN:(hc + 1) * WN],
                                     start=(hc == 0), stop=False,
                                     wait=((trow[hc % 2], th) +
                                           (bfree(min(dc, 6)) if hc == 0 else ())),
                                     inc=(dc == DC - 1))
                        t_hcdone[hc] = tk2
                    for dc in range(DC):
                        tk = mm(dn_bank(dc), idm[:], xa(dc, w), start=False,
                                stop=True, wait=(t_idb,), inc=True)
                        t_ffn = vop(lambda v, dc=dc, w=w: v.tensor_copy(
                            xb(dc, w), dn_bank(dc)), wait=(tk,))
                        bset(dc if dc < 6 else 6, t_ffn)
                    t_h_free = p.tok("mm")
                t_ln2 = layernorm(xb, xa, (t_ffn, p.tok("mm")))
                if debug:
                    dma("gpsimd", dbg["dx2"], xA[:], wait=(t_ln2, p.tok("mm"), p.tok("aa")))

            if phases >= 3:
                t_attn1 = attention(1, dep=(t_ln2, p.tok("mm")))
                t_ln3 = layernorm(xb, xa, (t_attn1, p.tok("mm")))
                if debug:
                    dma("gpsimd", dbg["dx3"], xA[:], wait=(t_ln3, p.tok("mm"), p.tok("aa")))

            if phases >= 4:
                tgw = load_slab(wsA, gw, 0, E)
                tgb = dma("sync", bcols[0:8, 20:21],
                          gb_i.rearrange("(a b) -> a b", b=1),
                          wait=(p.tok("vv"), p.tok("aa")))
                tsg = None
                for w in range(NW):
                    tk = None
                    for c in range(DC):
                        tk = mm(pb[4][0:8, :], slab_ap(wsA, E, c), xa(c, w),
                                start=(c == 0), stop=(c == DC - 1),
                                wait=((tgw, t_ln3) + bfree(4)) if c == 0 else (),
                                inc=(c == DC - 1))
                    tsg = aop(lambda sc, w=w: sc.activation(
                        sfcT[0:8, w * WN:(w + 1) * WN], pb[4][0:8, :],
                        AF.Sigmoid), wait=(tk, p.tok("mm")))
                    bset(4, tsg)
                tsb = vop(lambda v: v.tensor_scalar(
                    sfcT[0:8, :], sfcT[0:8, :], bcols[0:8, 20:21], None, ALU.add),
                    wait=(tsg, tgb))
                tk = None
                for tc in range(NT):
                    tk = mm(pb[5][:, 8 * tc:8 * tc + 8],
                            sfcT[0:8, tc * 128:(tc + 1) * 128], idf[0:8, 0:8],
                            start=True, stop=True,
                            wait=((tsb, t_id) + bfree(5)) if tc == 0 else (),
                            inc=(tc == NT - 1))
                tst = vop(lambda v: v.tensor_copy(sfctok[:, :], pb[5][:, 0:128]),
                          wait=(tk,))
                bset(5, tst)
                def chain(tc, R, R2):
                    st_ = sfctok[:, 8 * tc:8 * tc + 8]
                    st2 = st_.rearrange("p (g two) -> p g two", two=2)
                    ops = []
                    ops.append(lambda v: v.tensor_tensor(
                        R[:, 0:4], st2[:, :, 0:1].squeeze(2),
                        st2[:, :, 1:2].squeeze(2), ALU.add))
                    ops.append(lambda v: v.tensor_tensor(
                        R[:, 4:5], R[:, 0:1], R[:, 1:2], ALU.max))
                    ops.append(lambda v: v.tensor_tensor(
                        R[:, 5:6], R[:, 2:3], R[:, 3:4], ALU.max))
                    ops.append(lambda v: v.tensor_tensor(
                        R[:, 6:7], R[:, 0:1], R[:, 1:2], ALU.min))
                    ops.append(lambda v: v.tensor_tensor(
                        R[:, 7:8], R[:, 2:3], R[:, 3:4], ALU.min))
                    ops.append(lambda v: v.tensor_tensor(
                        R[:, 8:9], R[:, 4:5], R[:, 5:6], ALU.min))
                    ops.append(lambda v: v.tensor_tensor(
                        R[:, 9:10], R[:, 4:5], R[:, 5:6], ALU.is_ge))
                    ops.append(lambda v: v.tensor_tensor(
                        R[:, 10:11], R[:, 6:7], R[:, 7:8], ALU.subtract))
                    ops.append(lambda v: v.tensor_tensor(
                        R[:, 11:12], R[:, 9:10], R[:, 10:11], ALU.mult))
                    ops.append(lambda v: v.tensor_tensor(
                        R[:, 12:13], R[:, 7:8], R[:, 11:12], ALU.add))
                    ops.append(lambda v: v.tensor_tensor(
                        R[:, 13:14], R[:, 8:9], R[:, 12:13], ALU.max))
                    ops.append(lambda v: v.tensor_scalar(
                        R[:, 16:20], R[:, 0:4], R[:, 13:14], None, ALU.is_ge))
                    ops.append(lambda v: v.tensor_tensor(
                        R2[:, 0:8].rearrange("p (g two) -> p g two", two=2), st2,
                        R[:, 16:20].unsqueeze(2).broadcast_to([128, 4, 2]),
                        ALU.mult))
                    ops.append(lambda v: v.reduce_max(
                        R[:, 20:21], R2[:, 0:8], axis=AX.X))
                    ops.append(lambda v: v.tensor_scalar(
                        R2[:, 8:16], R2[:, 0:8], R[:, 20:21], None, ALU.is_equal))
                    ops.append(lambda v: v.scalar_tensor_tensor(
                        R2[:, 8:16], R2[:, 8:16], 1e30, R2[:, 0:8],
                        ALU.mult, ALU.subtract))
                    ops.append(lambda v: v.tensor_reduce(
                        R[:, 21:22], R2[:, 8:16], axis=AX.X, op=ALU.min))
                    ops.append(lambda v: v.tensor_scalar(
                        R[:, 22:23], R[:, 21:22], -1.0, None, ALU.mult))
                    ops.append(lambda v: v.tensor_tensor(
                        R[:, 23:24], R[:, 20:21], R[:, 22:23], ALU.add))
                    ops.append(lambda v: v.tensor_scalar(
                        R[:, 23:24], R[:, 23:24], 1e-20, None, ALU.add))
                    ops.append(lambda v: v.reciprocal(R[:, 24:25], R[:, 23:24]))
                    ops.append(lambda v: v.tensor_scalar(
                        R[:, 24:25], R[:, 24:25], 2.5, None, ALU.mult))
                    ops.append(lambda v: v.tensor_scalar(
                        R2[:, 0:8], R2[:, 0:8], R[:, 22:23], None, ALU.is_ge))
                    ops.append(lambda v, st_=st_: v.tensor_tensor(
                        R2[:, 0:8], R2[:, 0:8], st_, ALU.mult))
                    ops.append(lambda v, tc=tc: v.tensor_scalar(
                        ctok[:, 8 * tc:8 * tc + 8], R2[:, 0:8],
                        R[:, 24:25], None, ALU.mult))
                    return ops

                tct = None
                for tcp in range(8):
                    opsA = chain(2 * tcp, rscr, rscr2)
                    opsB = chain(2 * tcp + 1, rscrB, rscr2B)
                    for i in range(len(opsA)):
                        tct = vop(opsA[i], wait=(tst,) if i == 0 else ())
                        tct = vop(opsB[i], wait=(tst,) if i == 0 else ())
                if debug:
                    dma("gpsimd", dbg["dct"], ctok[:, :], wait=(tct,))
                    dma("gpsimd", dbg["dst2"], sfctok[:, :], wait=(tct,))

                def h_chunk_src(hc):
                    pass

                t_h_free = None
                t_gsb_free = None
                t_moe = None
                for w in range(NW):
                    for grp in range(2):
                        if grp == 0:
                            plan = ([("sh", 0, hc) for hc in range(8)] +
                                    [("ex", e, hc) for e in range(4)
                                     for hc in range(4)])
                        else:
                            plan = [("ex", e, hc) for e in range(4, 8)
                                    for hc in range(4)]
                        tg_l = tu_l = None
                        tg_n = tu_n = None
                        for i, (kind, e, hc) in enumerate(plan):
                            gbuf = wsA if i % 2 == 0 else wsC
                            ubuf = wsB if i % 2 == 0 else wsD
                            if i == 0:
                                if kind == "sh":
                                    tg_n = load_slab(gbuf, swg, hc * 128, 128)
                                    tu_n = load_slab(ubuf, swu, hc * 128, 128)
                                else:
                                    tg_n = load_slab(gbuf, wg1[e], hc * 128, 128)
                                    tu_n = load_slab(ubuf, wu1[e], hc * 128, 128)
                            tg_l, tu_l = tg_n, tu_n
                            if i + 1 < len(plan):
                                k2, e2, hc2 = plan[i + 1]
                                nb = wsC if i % 2 == 0 else wsA
                                nb2 = wsD if i % 2 == 0 else wsB
                                if k2 == "sh":
                                    tg_n = load_slab(nb, swg, hc2 * 128, 128)
                                    tu_n = load_slab(nb2, swu, hc2 * 128, 128)
                                else:
                                    tg_n = load_slab(nb, wg1[e2], hc2 * 128, 128)
                                    tu_n = load_slab(nb2, wu1[e2], hc2 * 128, 128)
                            tk = None
                            for c in range(DC):
                                tk = mm(pb[0][:, :], slab_ap(gbuf, 128, c),
                                        xa(c, w), start=(c == 0),
                                        stop=(c == DC - 1),
                                        wait=((tg_l, t_ln3) + bfree(0))
                                        if c == 0 else (),
                                        inc=(c == DC - 1))
                            tku = None
                            for c in range(DC):
                                tku = mm(pb[1][:, :], slab_ap(ubuf, 128, c),
                                         xa(c, w), start=(c == 0),
                                         stop=(c == DC - 1),
                                         wait=((tu_l, t_ln3) + bfree(1))
                                         if c == 0 else (),
                                         inc=(c == DC - 1))
                            tsl = aop(lambda sc: sc.activation(
                                gsb[:, :], pb[0][:, :], AF.Silu),
                                wait=(tk,) + bfree(0) +
                                ((t_gsb_free,) if t_gsb_free else ()))
                            bset(0, tsl)
                            dst_hc = hc if kind == "sh" else 8 + (e % 4) * 4 + hc
                            if kind == "sh":
                                th = vop(lambda v, dst_hc=dst_hc:
                                         v.scalar_tensor_tensor(
                                    hbuf[:, dst_hc * WN:(dst_hc + 1) * WN],
                                    pb[1][:, :], 1.0, gsb[:, :],
                                    ALU.mult, ALU.mult),
                                    wait=(tku, tsl) +
                                    ((t_h_free,) if t_h_free else ()))
                                bset(1, th)
                                t_gsb_free = th
                            else:
                                th0 = vop(lambda v: v.scalar_tensor_tensor(
                                    h32[:, :], pb[1][:, :], 1.0, gsb[:, :],
                                    ALU.mult, ALU.mult),
                                    wait=(tku, tsl))
                                bset(1, th0)
                                t_gsb_free = th0
                                if hc == 0:
                                    tk3 = None
                                    for ti in range(4):
                                        tk3 = mm(pb[5][0:1, ti * 128:(ti + 1) * 128],
                                                 ctok[:, 8 * (4 * w + ti) + e:
                                                      8 * (4 * w + ti) + e + 1],
                                                 idm[:], start=True, stop=True,
                                                 wait=((tct, t_idb) + bfree(5))
                                                 if ti == 0 else (),
                                                 inc=(ti == 3))
                                    tcr = vop(lambda v: v.tensor_copy(
                                        rowB[0:1, 0:WN], pb[5][0:1, 0:WN]),
                                        wait=(tk3,))
                                    bset(5, tcr)
                                    trc1 = mm(pb[4][:, :], ones_r[0:1, 0:128],
                                              rowB[0:1, 0:WN],
                                              start=True, stop=True,
                                              wait=(tcr,) + bfree(4), inc=True)
                                th = vop(lambda v, dst_hc=dst_hc: v.tensor_tensor(
                                    hbuf[:, dst_hc * WN:(dst_hc + 1) * WN],
                                    h32[:, :], pb[4][:, :], ALU.mult),
                                    wait=(th0, trc1) +
                                    ((t_h_free,) if t_h_free else ()))
                                bset(4, th)
                        nhc = 24 if grp == 0 else 16
                        def dn_bank2(dc):
                            if dc < 6:
                                return pb[dc][:, :]
                            return pbig[:, (dc - 6) * 512:(dc - 5) * 512]
                        def dn_src(j):
                            if grp == 0 and j < 8:
                                return swd.rearrange("(hc p) d -> p hc d",
                                                     p=128)[:, j, :]
                            jj = j - 8 if grp == 0 else j
                            e = (jj // 4) + (0 if grp == 0 else 4)
                            ic = jj % 4
                            return wd1[e].rearrange("(ic p) d -> p ic d",
                                                    p=128)[:, ic, :]
                        trow = [None, None]
                        t_hcd = [None] * nhc
                        trow[0] = dma("gpsimd", wsC[:, 0:1024], dn_src(0),
                                      wait=(p.tok("mm"),))
                        for j in range(nhc):
                            buf = wsC if j % 2 == 0 else wsD
                            if j + 1 < nhc:
                                trow[(j + 1) % 2] = dma(
                                    "gpsimd", (wsD if j % 2 == 0 else wsC)[:, 0:1024],
                                    dn_src(j + 1),
                                    wait=(p.tok("mm"),) +
                                    ((t_hcd[j - 1],) if j >= 1 else ()))
                            src_hc = j if grp == 0 else 8 + j
                            for dc in range(DC):
                                tk2 = mm(dn_bank2(dc),
                                         buf[:, dc * 128:(dc + 1) * 128],
                                         hbuf[:, src_hc * WN:(src_hc + 1) * WN],
                                         start=(j == 0),
                                         stop=(grp == 1 and j == nhc - 1),
                                         wait=((trow[j % 2], th) +
                                               (bfree(min(dc, 6)) if j == 0
                                                else ())),
                                         inc=(dc == DC - 1))
                            t_hcd[j] = tk2
                        if grp == 0:
                            for dc in range(DC):
                                tk2 = mm(dn_bank2(dc), idm[:], xa(dc, w),
                                         start=False, stop=True,
                                         wait=(t_idb,), inc=True)
                                tev = vop(lambda v, dc=dc: v.tensor_copy(
                                    accbuf[:, dc * WN:(dc + 1) * WN],
                                    dn_bank2(dc)), wait=(tk2,))
                                bset(dc if dc < 6 else 6, tev)
                                t_acc = tev
                        else:
                            for dc in range(DC):
                                tev = vop(lambda v, dc=dc, w=w:
                                          v.scalar_tensor_tensor(
                                    xb(dc, w), dn_bank2(dc), 1.0,
                                    accbuf[:, dc * WN:(dc + 1) * WN],
                                    ALU.mult, ALU.add),
                                    wait=(t_hcd[nhc - 1], t_acc))
                                bset(dc if dc < 6 else 6, tev)
                                t_moe = tev
                        t_h_free = p.tok("mm")
                t_ln4s = layernorm(xb, xa, (t_moe, p.tok("mm")), apply=False)

            t_fin = t_ln
            if phases >= 2:
                t_fin = t_ln2
            if phases >= 3:
                t_fin = t_ln3

            if debug:
                wd = (p.tok("vv"), p.tok("aa"), p.tok("mm"))
                dma("gpsimd", dbg["dxA"], xA[:], wait=wd)
                dma("gpsimd", dbg["dxB"], xB[:], wait=wd)
                dma("gpsimd", dbg["dq"], qp[:], wait=wd)
                dma("gpsimd", dbg["dk"], kp[:], wait=wd)
                dma("gpsimd", dbg["dv"], v65[:], wait=wd)
                dma("gpsimd", dbg["dctx"], ctxall[:], wait=wd)
                dma("gpsimd", dbg["dhb"], hbuf[:], wait=wd)
                dma("gpsimd", dbg["drw"], sfcT[:], wait=wd)
                if phases >= 2:
                    dma("gpsimd", dbg["dwt"], wg0T[0:256, :], wait=wd + (t_w0T,))

            tlast = None
            st_free = None
            if phases >= 4:
                for tc in range(NT):
                    tmu = mm(pb[4][:, 0:1], rowC[0:1, tc * 128:(tc + 1) * 128],
                             ones_r[0:1, 0:1], start=True, stop=True,
                             wait=(t_ln4s, t_ones) + bfree(4))
                    tmu = mm(pb[4][:, 1:2], rowA[0:1, tc * 128:(tc + 1) * 128],
                             ones_r[0:1, 0:1], start=True, stop=True, inc=True)
                    tsc2 = vop(lambda v: v.tensor_copy(rscr[:, 0:2],
                                                       pb[4][:, 0:2]),
                               wait=(tmu,))
                    bset(4, tsc2)
                    tk = None
                    for c in range(DC):
                        tk = mm(pbig[:, c * 128:(c + 1) * 128],
                                xB[:, c * S + tc * 128: c * S + (tc + 1) * 128],
                                idm[:], start=True, stop=True,
                                wait=((t_idb, t_ln4s) + bfree(6) +
                                      (p.tok("vv"),)) if c == 0 else (),
                                inc=(c == DC - 1))
                    tev = vop(lambda v: v.tensor_scalar(
                        outt[:, :], pbig[:, :], rscr[:, 0:1], rscr[:, 1:2],
                        ALU.add, ALU.mult),
                        wait=(tk, tsc2) + ((st_free,) if st_free else ()))
                    bset(6, tev)
                    tlast = dma("sync", y_out[tc * 128:(tc + 1) * 128, :],
                                outt[:, :], wait=(tev,))
                    st_free = tlast
            else:
                for tc in range(NT):
                    tk = None
                    for c in range(DC):
                        tk = mm(pbig[:, c * 128:(c + 1) * 128],
                                xA[:, c * S + tc * 128: c * S + (tc + 1) * 128],
                                idm[:], start=True, stop=True,
                                wait=((t_idb, t_fin) + bfree(6) + (p.tok("vv"),))
                                if c == 0 else (),
                                inc=(c == DC - 1))
                    tev = vop(lambda v, tc=tc: v.tensor_copy(
                        outt[:, :], pbig[:, :]),
                        wait=(tk,) + ((st_free,) if st_free else ()))
                    bset(6, tev)
                    tlast = dma("sync", y_out[tc * 128:(tc + 1) * 128, :],
                                outt[:, :], wait=(tev,))
                    st_free = tlast
            p.do("vector", lambda v: v.tensor_copy(cstr[0:1, 8:9], cstr[0:1, 0:1]),
                 wait=(tlast,))

        def run_for(name):
            def body(eng):
                program(Prog(sems, active=name, eng=eng))
            return body

        block.sync(run_for("sync"))
        block.tensor(run_for("tensor"))
        block.vector(run_for("vector"))
        block.scalar(run_for("scalar"))
        block.gpsimd(run_for("gpsimd"))

    return nc


def make_in_maps(inputs):
    eye = np.eye(128, dtype=np.float32)
    sh64 = np.eye(64, 128, 64, dtype=np.float32)
    sel = np.zeros((129, 384), dtype=np.float32)
    for c in range(3):
        for j in range(128):
            idxv = min(max(128 * c + j - 64, 0), 128)
            sel[idxv, 128 * c + j] += 1.0
            sel[0, 128 * c + j] -= 1.0
    maps = []
    for b in range(8):
        m = {
            "x": inputs["L0_attn_x"][b],
            "Wq": inputs["L0_attn_Wq"], "bq": inputs["L0_attn_bq"],
            "Wl": inputs["L0_attn_Wl"], "bl": inputs["L0_attn_bl"],
            "Wk": inputs["L0_attn_Wk"],
            "Wv": inputs["L0_attn_Wv"], "bv": inputs["L0_attn_bv"],
            "Wo": inputs["L0_attn_Wo"], "bo": inputs["L0_attn_bo"],
            "wg0": inputs["L0_ffn_w_gate"], "wu0": inputs["L0_ffn_w_up"],
            "wd0": inputs["L0_ffn_w_down"],
            "Wqkv": inputs["L1_attn_W_qkv"], "bqkv": inputs["L1_attn_b_qkv"],
            "Wout": inputs["L1_attn_W_out"], "bout": inputs["L1_attn_b_out"],
            "relk": inputs["L1_attn_rel_emb_k"], "relv": inputs["L1_attn_rel_emb_v"],
            "gw": inputs["L1_ffn_gate_w"], "gb": inputs["L1_ffn_gate_bias"],
            "wg1": inputs["L1_ffn_w_gate"], "wu1": inputs["L1_ffn_w_up"],
            "wd1": inputs["L1_ffn_w_down"],
            "swg": inputs["L1_ffn_sw_gate"], "swu": inputs["L1_ffn_sw_up"],
            "swd": inputs["L1_ffn_sw_down"],
            "idm": eye, "sh64": sh64, "selw": sel,
        }
        maps.append({k: np.ascontiguousarray(v) for k, v in m.items()})
    return maps


def kernel(**inputs):
    nc = build_nc(debug=0)
    maps = make_in_maps(inputs)
    res = run_bass_kernel_spmd(nc, maps, core_ids=list(range(8)))
    out = np.stack([np.asarray(res.results[b]["y"]) for b in range(8)], axis=0)
    return out.astype(np.float32)
